# version 02
# baseline (speedup 1.0000x reference)
import numpy as np
import concourse.bass as bass
import concourse.mybir as mybir
from concourse import library_config
from concourse.bass_utils import run_bass_kernel_spmd
from concourse.alu_op_type import AluOpType as Op

F32 = mybir.dt.float32
BF16 = mybir.dt.bfloat16
I32 = mybir.dt.int32
AF = mybir.ActivationFunctionType
AX = mybir.AxisListType

NCORES = 8
S = 2048
F = 2048
DFF = 8192
T = 4096

def _split12(x):
    import math
    m, e = math.frexp(x)
    scale = 2.0 ** (e - 12)
    return float(np.float32(round(x / scale) * scale))
_TWO_PI = 6.283185307179586476925287
CW_C1 = _split12(_TWO_PI)
CW_C2 = _split12(_TWO_PI - CW_C1)
CW_C3 = float(np.float32(_TWO_PI - CW_C1 - CW_C2))
INV_2PI = float(np.float32(1.0 / _TWO_PI))
INV_SQRT_HD = float(np.float32(1.0 / np.sqrt(np.float32(128.0))))
HALF_PI = float(np.float32(np.pi / 2))
PI_F = float(np.float32(np.pi))
TWO_PI_F = float(np.float32(2 * np.pi))


def tau_perm(g):
    s = np.arange(S)
    return 2048 * g + 2 * (s % 1024) + (s // 1024)


class Ctx:

    def __init__(self, nc, block):
        self.nc = nc
        self.block = block
        self._sem_cm = []
        self.sems = {}
        self.cnt = {}
        self.marks = {}

    def sem(self, name):
        if name not in self.sems:
            cm = self.nc.semaphore(name)
            h = cm.__enter__()
            self._sem_cm.append(cm)
            self.sems[name] = h
            self.cnt[name] = 0
        return self.sems[name]

    def inc(self, inst, name, n=1):
        inst.then_inc(self.sem(name), n)
        self.cnt[name] += n
        return self.cnt[name]

    def mark(self, label, name):
        self.marks[label] = (name, self.cnt[name])

    def wait(self, eng, label):
        name, v = self.marks[label]
        if v > 0:
            eng.wait_ge(self.sems[name], v)

    def wait_cnt(self, eng, name):
        if self.cnt.get(name, 0) > 0:
            eng.wait_ge(self.sem(name), self.cnt[name])

    def barrier(self):
        names = [n for n in self.cnt if self.cnt[n] > 0]
        def mk(eng):
            for n in names:
                eng.wait_ge(self.sems[n], self.cnt[n])
        self.block.sync(mk)
        self.block.tensor(mk)
        self.block.vector(mk)
        self.block.scalar(mk)
        self.block.gpsimd(mk)

    def vec(self, fn):
        def wrap(v):
            self.wait_cnt(v, "cv")
            fn(v)
        self.block.vector(wrap)

    def scl(self, fn):
        def wrap(s):
            self.wait_cnt(s, "cs")
            fn(s)
        self.block.scalar(wrap)

    def gp(self, fn):
        def wrap(g):
            self.wait_cnt(g, "cg")
            fn(g)
        self.block.gpsimd(wrap)


def build_nc(phases=("A", "B", "D", "E")):
    nc = bass.Bass("TRN2", target_bir_lowering=False, debug=False)

    ap = {}
    def din(name, shape, dtype=F32):
        ap[name] = nc.dram_tensor(name, list(shape), dtype, kind="ExternalInput").ap()
        return ap[name]

    din("xt", (F, T))
    din("xres", (S, F))
    din("wqcol", (F, 1024))
    din("hhb", (1, 1024))
    din("gnw", (1, 1024))
    din("gnb", (1, 1024))
    din("ropec", (T, 32))
    din("ropes", (T, 32))
    din("w1t", (F, DFF))
    din("w2t", (DFF, F))
    din("g1r", (1, F))
    din("g2r", (1, DFF))
    din("wqt", (F, F))
    din("wkt", (F, F))
    din("wvt", (F, F))
    din("wot", (F, F))
    din("invf", (1, 64))
    din("posr", (1, S))
    din("ident", (128, 128))
    din("ones128", (128, 1))

    out = nc.dram_tensor("out", [S, F], F32, kind="ExternalOutput").ap()

    xstream = nc.dram_tensor("xstream", [S, F], F32).ap()
    m1q = nc.dram_tensor("m1q", [F, DFF], BF16).ap()
    m2q = nc.dram_tensor("m2q", [DFF, F], BF16).ap()
    th_scr = nc.dram_tensor("th_scr", [S, DFF], F32).ap()
    q_scr = nc.dram_tensor("q_scr", [S, F], F32).ap()
    k_scr = nc.dram_tensor("k_scr", [S, F], F32).ap()
    v_scr = nc.dram_tensor("v_scr", [S, F], F32).ap()
    at_scr = nc.dram_tensor("at_scr", [F, S], F32).ap()

    with (
        nc.sbuf_tensor("identt", [128, 128], F32) as identt,
        nc.sbuf_tensor("identb", [128, 128], BF16) as identb,
        nc.sbuf_tensor("ones_t", [128, 1], F32) as ones_t,
        nc.psum_tensor("p0", [128, 512], F32) as p0,
        nc.psum_tensor("p1", [128, 512], F32) as p1,
        nc.psum_tensor("p2", [128, 512], F32) as p2,
        nc.psum_tensor("p3", [128, 512], F32) as p3,
        nc.psum_tensor("p4", [128, 512], F32) as p4,
        nc.psum_tensor("p5", [128, 512], F32) as p5,
        nc.psum_tensor("pb6", [128, 1024], BF16) as pb6,
        nc.psum_tensor("pb7", [128, 1024], BF16) as pb7,
        nc.sbuf_tensor("pers_sc", [128, 8], F32) as pers_sc,
        nc.sbuf_tensor("cst_hpi", [128, 1], F32) as cst_hpi,
        nc.Block() as block,
    ):
        ctx = Ctx(nc, block)
        ctx.P = [p0, p1, p2, p3, p4, p5]
        ctx.PB = [pb6, pb7]
        ctx.pers_sc = pers_sc
        ctx.identb = identb
        ctx.identt = identt
        ctx.ones_t = ones_t
        ctx.ap = ap
        ctx.out = out
        ctx.xstream = xstream
        ctx.scr = dict(m1q=m1q, m2q=m2q, th_scr=th_scr, q_scr=q_scr,
                       k_scr=k_scr, v_scr=v_scr, at_scr=at_scr)
        for s_ in ("cv", "cs", "ct", "cg", "dout0", "dout1", "dout2", "dout3", "dfin"):
            ctx.sem(s_)
        def wait_douts(eng):
            for s_ in ("dout0", "dout1", "dout2", "dout3"):
                ctx.wait_cnt(eng, s_)
        ctx.wait_douts = wait_douts

        def _init(sync):
            ctx.inc(sync.dma_start(out=identt[:], in_=ap["ident"]), "dini", 16)
            ctx.inc(sync.dma_start(out=ones_t[:], in_=ap["ones128"]), "dini", 16)
            ctx.mark("init_done", "dini")
        ctx.sem("dini")
        block.sync(_init)

        def _init_g(g):
            g.load_library(library_config.proxy)
        ctx.gp(_init_g)

        def _init_v(v):
            ctx.wait(v, "init_done")
            ctx.inc(v.tensor_copy(identb[:], identt[:]), "cv")
            ctx.mark("identb_done", "cv")
            ctx.inc(v.memset(cst_hpi[:], HALF_PI), "cv")
            ctx.mark("hpi_done", "cv")
        ctx.vec(_init_v)
        ctx.cst_hpi = cst_hpi

        if "A" in phases:
            emit_phase_a(ctx)
        if "B" in phases:
            emit_ffn(ctx, tag="B")
        if "D" in phases:
            emit_attn_b(ctx)
        if "E" in phases:
            emit_ffn(ctx, tag="E")
        elif DUMP is None:
            def _cp(sync):
                for e in ("cv", "cs", "ct", "cg"):
                    ctx.wait_cnt(sync, e)
                ctx.wait_douts(sync)
                ctx.inc(sync.dma_start(out=out, in_=xstream), "dfin", 16)
            block.sync(_cp)

        def _fin(sync):
            ctx.wait_cnt(sync, "dfin")
            ctx.wait_douts(sync)
        block.sync(_fin)

    from concourse.library_overlay import lower_extended_insts
    lower_extended_insts(nc)
    return nc


def emit_phase_a(ctx):
    nc, block = ctx.nc, ctx.block
    ap = ctx.ap
    P = ctx.P
    identt = ctx.identt
    ones_t = ctx.ones_t
    NG = 8

    from contextlib import ExitStack
    with ExitStack() as stack:
        def sb(name, shape, dtype=F32):
            return stack.enter_context(nc.sbuf_tensor(name, list(shape), dtype))
        xt0 = sb("a_xt0", [128, 16 * 512])
        xt1 = sb("a_xt1", [128, 16 * 512])
        xsq = sb("a_xsq", [128, 16 * 512])
        wq = sb("a_wq", [128, 16 * 1024])
        csrow = sb("a_csrow", [1, 1024])
        srow = sb("a_srow", [1, 512])
        qrow = sb("a_qrow", [1, 512])
        mrow = sb("a_mrow", [1, 512])
        vrow = sb("a_vrow", [1, 512])
        invs = sb("a_invs", [128, 4])
        hhb_b = sb("a_hhb", [128, 1024])
        gnw_b = sb("a_gnw", [128, 1024])
        gnb_b = sb("a_gnb", [128, 1024])
        ropec = sb("a_ropec", [128, 4 * 32])
        ropes = sb("a_ropes", [128, 4 * 32])
        wk1 = sb("a_wk1", [128, 512])
        wk2 = sb("a_wk2", [128, 512])
        wk3 = sb("a_wk3", [128, 512])
        wk4 = sb("a_wk4", [128, 512])
        wk5 = sb("a_wk5", [128, 512])
        gst = sb("a_gst", [128, 48])
        rtmp = sb("a_rtmp", [128, 64])
        xres4 = sb("a_xres", [128, 4 * 128])
        ob4 = sb("a_ob", [128, 4 * 128])
        xtb = [xt0, xt1]

        def a_init_sync(sync):
            ctx.inc(sync.dma_start(out=wq[:].rearrange("p (k n) -> p k n", n=1024),
                                   in_=ap["wqcol"].rearrange("(k p) n -> p k n", p=128)),
                    "dwq", 16)
            ctx.inc(sync.dma_start(out=hhb_b[0:1, :], in_=ap["hhb"]), "dwq", 16)
            ctx.inc(sync.dma_start(out=gnw_b[0:1, :], in_=ap["gnw"]), "dwq", 16)
            ctx.inc(sync.dma_start(out=gnb_b[0:1, :], in_=ap["gnb"]), "dwq", 16)
            ctx.mark("a_wq_loaded", "dwq")
        ctx.sem("dwq")
        block.sync(a_init_sync)

        def a_init_g(g):
            ctx.wait(g, "a_wq_loaded")
            ctx.inc(g.partition_broadcast(hhb_b[:], hhb_b[0:1, :]), "cg")
            ctx.inc(g.partition_broadcast(gnw_b[:], gnw_b[0:1, :]), "cg")
            ctx.inc(g.partition_broadcast(gnb_b[:], gnb_b[0:1, :]), "cg")
            ctx.mark("a_bcast", "cg")
        ctx.gp(a_init_g)

        def a_colsum_t(t):
            ctx.wait(t, "init_done")
            ctx.wait(t, "a_wq_loaded")
            for half in range(2):
                bank = P[2 + half]
                for k in range(16):
                    ctx.inc(t.matmul(bank[0:1, 0:512], ones_t[:],
                                     wq[:, k * 1024 + half * 512: k * 1024 + half * 512 + 512],
                                     start=(k == 0), stop=(k == 15)), "ct")
                ctx.mark("a_cs_%d" % half, "ct")
        block.tensor(a_colsum_t)

        def a_colsum_v(v):
            for half in range(2):
                ctx.wait(v, "a_cs_%d" % half)
                ctx.inc(v.tensor_copy(csrow[0:1, half * 512:half * 512 + 512],
                                      P[2 + half][0:1, 0:512]), "cv")
            ctx.mark("a_csrow", "cv")
        ctx.vec(a_colsum_v)

        for s_ in ("dxt0", "dxt1", "drp", "dxr0", "dxr1", "dxr2", "dxr3"):
            ctx.sem(s_)

        for gi in range(NG):
            par = gi % 2
            xt = xtb[par]
            h = 1 if gi >= 4 else 0

            def a_load(sync, gi=gi, par=par, xt=xt):
                if gi >= 2:
                    ctx.wait(sync, "a_mm_%d" % ((gi - 2) * 8 + 7))
                if gi >= 1:
                    ctx.wait(sync, "a_pre_tanh_%d" % ((gi - 1) * 8 + 6))
                src = ap["xt"].rearrange("(k p) t -> p k t", p=128)
                ctx.inc(sync.dma_start(
                    out=xt[:].rearrange("p (k n) -> p k n", n=512),
                    in_=src[:, :, gi * 512:(gi + 1) * 512]), "dxt%d" % par, 16)
                ctx.mark("a_xt_%d" % gi, "dxt%d" % par)
                ctx.inc(sync.dma_start(
                    out=ropec[:].rearrange("p (a n) -> p a n", n=32),
                    in_=ap["ropec"].rearrange("(a p) n -> p a n", p=128)[:, 4 * gi:4 * gi + 4, :]),
                    "drp", 16)
                ctx.inc(sync.dma_start(
                    out=ropes[:].rearrange("p (a n) -> p a n", n=32),
                    in_=ap["ropes"].rearrange("(a p) n -> p a n", p=128)[:, 4 * gi:4 * gi + 4, :]),
                    "drp", 16)
                ctx.mark("a_rp_%d" % gi, "drp")
            block.sync(a_load)

            def a_sq(sc, gi=gi, xt=xt):
                ctx.wait(sc, "a_xt_%d" % gi)
                if gi >= 1:
                    ctx.wait(sc, "a_stat_%d" % (gi - 1))
                ctx.inc(sc.activation(xsq[:], xt[:], AF.Square), "cs")
                ctx.mark("a_sq_%d" % gi, "cs")
            ctx.scl(a_sq)

            def a_stat(t, gi=gi, xt=xt):
                ctx.wait(t, "a_xt_%d" % gi)
                ctx.wait(t, "a_sq_%d" % gi)
                if gi == 0:
                    ctx.wait(t, "a_csrow")
                else:
                    ctx.wait(t, "a_rowe_%d" % (gi - 1))
                for k in range(16):
                    ctx.inc(t.matmul(P[2][0:1, 0:512], ones_t[:],
                                     xt[:, k * 512:(k + 1) * 512],
                                     start=(k == 0), stop=(k == 15)), "ct")
                for k in range(16):
                    ctx.inc(t.matmul(P[3][0:1, 0:512], ones_t[:],
                                     xsq[:, k * 512:(k + 1) * 512],
                                     start=(k == 0), stop=(k == 15)), "ct")
                ctx.mark("a_stat_%d" % gi, "ct")
            block.tensor(a_stat)

            def a_rows_v(v, gi=gi):
                ctx.wait(v, "a_stat_%d" % gi)
                if gi >= 1:
                    ctx.wait(v, "a_mm_%d" % (gi * 8 - 1))
                    ctx.wait(v, "a_tinv_%d" % (gi - 1))
                ctx.inc(v.tensor_copy(srow[0:1, :], P[2][0:1, 0:512]), "cv")
                ctx.inc(v.tensor_copy(qrow[0:1, :], P[3][0:1, 0:512]), "cv")
                ctx.mark("a_rowe_%d" % gi, "cv")
                ctx.wait_cnt(v, "cv")
                ctx.inc(v.tensor_scalar(mrow[0:1, :], srow[0:1, :], -1.0 / F, None, Op.mult), "cv")
                ctx.inc(v.tensor_scalar(vrow[0:1, :], srow[0:1, :], 1.0 / F, None, Op.mult), "cv")
                ctx.wait_cnt(v, "cv")
                ctx.inc(v.tensor_tensor(vrow[0:1, :], vrow[0:1, :], vrow[0:1, :], Op.mult), "cv")
                ctx.wait_cnt(v, "cv")
                ctx.inc(v.scalar_tensor_tensor(vrow[0:1, :], qrow[0:1, :], 1.0 / F,
                                               vrow[0:1, :], Op.mult, Op.subtract), "cv")
                ctx.wait_cnt(v, "cv")
                ctx.inc(v.tensor_scalar(vrow[0:1, :], vrow[0:1, :], 1e-5, None, Op.add), "cv")
                ctx.wait_cnt(v, "cv")
                ctx.inc(v.reciprocal(vrow[0:1, :], vrow[0:1, :]), "cv")
                ctx.mark("a_recip_%d" % gi, "cv")
            ctx.vec(a_rows_v)

            def a_rows_s(sc, gi=gi):
                ctx.wait(sc, "a_recip_%d" % gi)
                ctx.inc(sc.activation(vrow[0:1, :], vrow[0:1, :], AF.Sqrt), "cs")
                ctx.mark("a_invs_row_%d" % gi, "cs")
            ctx.scl(a_rows_s)

            def a_tinv(t, gi=gi):
                ctx.wait(t, "a_invs_row_%d" % gi)
                if gi >= 1:
                    ctx.wait(t, "a_inve_%d" % (gi - 1))
                for a_ in range(4):
                    ctx.inc(t.transpose(P[4][:, a_:a_ + 1],
                                        vrow[0:1, a_ * 128:(a_ + 1) * 128],
                                        identt[0:1, 0:1]), "ct")
                ctx.mark("a_tinv_%d" % gi, "ct")
            block.tensor(a_tinv)

            def a_einv(v, gi=gi):
                ctx.wait(v, "a_tinv_%d" % gi)
                if gi >= 1:
                    ctx.wait(v, "a_ev1_%d" % (gi * 8 - 1))
                ctx.inc(v.tensor_copy(invs[:], P[4][:, 0:4]), "cv")
                ctx.mark("a_inve_%d" % gi, "cv")
            ctx.vec(a_einv)

            for a_loc in range(4):
                a = gi * 4 + a_loc
                for pc in range(2):
                    m = a * 2 + pc
                    bank = P[0]
                    tb = P[1] if m % 2 == 0 else P[5]

                    def a_mm(t, gi=gi, a_loc=a_loc, pc=pc, m=m, bank=bank, xt=xt):
                        ctx.wait(t, "a_inve_%d" % gi)
                        ctx.wait(t, "a_csrow")
                        if m >= 1:
                            ctx.wait(t, "a_ev1_%d" % (m - 1))
                        for k in range(16):
                            ctx.inc(t.matmul(
                                bank[:],
                                xt[:, k * 512 + a_loc * 128: k * 512 + (a_loc + 1) * 128],
                                wq[:, k * 1024 + pc * 512: k * 1024 + (pc + 1) * 512],
                                start=(k == 0), stop=False), "ct")
                        ctx.inc(t.matmul(
                            bank[:], mrow[0:1, a_loc * 128:(a_loc + 1) * 128],
                            csrow[0:1, pc * 512:(pc + 1) * 512],
                            start=False, stop=True), "ct")
                        ctx.mark("a_mm_%d" % m, "ct")
                    block.tensor(a_mm)

                    def a_ep_v1(v, gi=gi, a_loc=a_loc, pc=pc, m=m, bank=bank):
                        ctx.wait(v, "a_mm_%d" % m)
                        ctx.wait(v, "a_bcast")
                        if m >= 1:
                            ctx.wait(v, "a_tanh_%d" % (m - 1))
                        ctx.inc(v.tensor_scalar(wk1[:], bank[:], invs[:, a_loc:a_loc + 1],
                                                None, Op.mult), "cv")
                        ctx.mark("a_ev1_%d" % m, "cv")
                        if pc == 0:
                            ctx.wait(v, "a_rp_%d" % gi)
                            co = ropec[:, a_loc * 32:(a_loc + 1) * 32]
                            si = ropes[:, a_loc * 32:(a_loc + 1) * 32]
                            xe = wk1[:, 0:64:2]
                            xo = wk1[:, 1:64:2]
                            ctx.wait_cnt(v, "cv")
                            ctx.inc(v.tensor_tensor(rtmp[:, 0:32], xe, co, Op.mult), "cv")
                            ctx.inc(v.tensor_tensor(rtmp[:, 32:64], xo, si, Op.mult), "cv")
                            ctx.inc(v.tensor_tensor(wk2[:, 0:32], xo, co, Op.mult), "cv")
                            ctx.inc(v.tensor_tensor(wk2[:, 32:64], xe, si, Op.mult), "cv")
                            ctx.wait_cnt(v, "cv")
                            ctx.inc(v.tensor_tensor(wk1[:, 0:64:2], rtmp[:, 0:32],
                                                    rtmp[:, 32:64], Op.subtract), "cv")
                            ctx.inc(v.tensor_tensor(wk1[:, 1:64:2], wk2[:, 0:32],
                                                    wk2[:, 32:64], Op.add), "cv")
                        ctx.wait_cnt(v, "cv")
                        ctx.inc(v.scalar_tensor_tensor(
                            wk2[:], wk1[:], 0.0, hhb_b[:, pc * 512:(pc + 1) * 512],
                            Op.max, Op.add), "cv")
                        ctx.mark("a_pre_tanh_%d" % m, "cv")
                    ctx.vec(a_ep_v1)

                    def a_ep_s1(sc, m=m):
                        ctx.wait(sc, "a_pre_tanh_%d" % m)
                        if m >= 1:
                            ctx.wait(sc, "a_gn_%d" % (m - 1))
                        ctx.inc(sc.activation(wk3[:], wk2[:], AF.Tanh), "cs")
                        ctx.mark("a_tanh_%d" % m, "cs")
                        ctx.wait_cnt(sc, "cs")
                        ctx.inc(sc.activation(wk4[:], wk3[:], AF.Square), "cs")
                        ctx.mark("a_gnsq_%d" % m, "cs")
                    ctx.scl(a_ep_s1)

                    def a_ep_v2(v, m=m):
                        ctx.wait(v, "a_gnsq_%d" % m)
                        w3v = wk3[:].rearrange("p (g d) -> p g d", d=64)
                        w4v = wk4[:].rearrange("p (g d) -> p g d", d=64)
                        ctx.inc(v.tensor_reduce(gst[:, 0:8], w3v, AX.X, Op.add), "cv")
                        ctx.inc(v.tensor_reduce(gst[:, 8:16], w4v, AX.X, Op.add), "cv")
                        ctx.wait_cnt(v, "cv")
                        ctx.inc(v.tensor_scalar(gst[:, 16:24], gst[:, 0:8], 1.0 / 64, None, Op.mult), "cv")
                        ctx.wait_cnt(v, "cv")
                        ctx.inc(v.tensor_tensor(gst[:, 24:32], gst[:, 16:24], gst[:, 16:24], Op.mult), "cv")
                        ctx.wait_cnt(v, "cv")
                        ctx.inc(v.scalar_tensor_tensor(gst[:, 24:32], gst[:, 8:16], 1.0 / 64,
                                                       gst[:, 24:32], Op.mult, Op.subtract), "cv")
                        ctx.wait_cnt(v, "cv")
                        ctx.inc(v.tensor_scalar(gst[:, 24:32], gst[:, 24:32], 1e-5, None, Op.add), "cv")
                        ctx.wait_cnt(v, "cv")
                        ctx.inc(v.reciprocal(gst[:, 32:40], gst[:, 24:32]), "cv")
                        ctx.mark("a_gnr_%d" % m, "cv")
                    ctx.vec(a_ep_v2)

                    def a_ep_s2(sc, m=m):
                        ctx.wait(sc, "a_gnr_%d" % m)
                        ctx.inc(sc.activation(gst[:, 40:48], gst[:, 32:40], AF.Sqrt), "cs")
                        ctx.mark("a_gninv_%d" % m, "cs")
                    ctx.scl(a_ep_s2)

                    def a_ep_v3(v, pc=pc, m=m):
                        ctx.wait(v, "a_gninv_%d" % m)
                        if m >= 1:
                            ctx.wait(v, "a_tp_%d" % (m - 1))
                        for j in range(8):
                            ctx.inc(v.tensor_scalar(
                                wk5[:, j * 64:(j + 1) * 64], wk3[:, j * 64:(j + 1) * 64],
                                gst[:, 16 + j:17 + j], gst[:, 40 + j:41 + j],
                                Op.subtract, Op.mult), "cv")
                        ctx.wait_cnt(v, "cv")
                        ctx.inc(v.tensor_tensor(wk5[:], wk5[:],
                                                gnw_b[:, pc * 512:(pc + 1) * 512], Op.mult), "cv")
                        ctx.wait_cnt(v, "cv")
                        ctx.inc(v.tensor_tensor(wk5[:], wk5[:],
                                                gnb_b[:, pc * 512:(pc + 1) * 512], Op.add), "cv")
                        ctx.mark("a_gn_%d" % m, "cv")
                    ctx.vec(a_ep_v3)

                    def a_tp(t, m=m, tb=tb):
                        ctx.wait(t, "a_gn_%d" % m)
                        if m >= 2:
                            ctx.wait(t, "a_res_%d" % ((m - 2) * 4 + 3))
                        for cs_ in range(4):
                            ctx.inc(t.transpose(tb[:, cs_ * 128:(cs_ + 1) * 128],
                                                wk5[:, cs_ * 128:(cs_ + 1) * 128],
                                                identt[:]), "ct")
                        ctx.mark("a_tp_%d" % m, "ct")
                    block.tensor(a_tp)

                    def a_out_ld(sync, a=a, pc=pc, m=m, h=h):
                        j = 128 * (a % 16)
                        for cs_ in range(4):
                            idx = m * 4 + cs_
                            st = 4 * pc + cs_ + 8 * h
                            if idx >= 4:
                                ctx.wait(sync, "a_res_%d" % (idx - 4))
                            ctx.inc(sync.dma_start(
                                out=xres4[:, cs_ * 128:(cs_ + 1) * 128],
                                in_=ctx.ap["xres"][st * 128:(st + 1) * 128, j:j + 128]),
                                "dxr%d" % cs_, 16)
                            ctx.mark("a_xr_%d" % idx, "dxr%d" % cs_)
                    block.sync(a_out_ld)

                    def a_out_v(v, m=m, tb=tb):
                        ctx.wait(v, "a_tp_%d" % m)
                        for cs_ in range(4):
                            idx = m * 4 + cs_
                            ctx.wait(v, "a_xr_%d" % idx)
                            if idx >= 4:
                                ctx.wait(v, "a_ost_%d" % (idx - 4))
                            ctx.inc(v.scalar_tensor_tensor(
                                ob4[:, cs_ * 128:(cs_ + 1) * 128],
                                tb[:, cs_ * 128:(cs_ + 1) * 128], 1.0,
                                xres4[:, cs_ * 128:(cs_ + 1) * 128],
                                Op.mult, Op.add), "cv")
                            ctx.mark("a_res_%d" % idx, "cv")
                    ctx.vec(a_out_v)

                    def a_out_st(sync, a=a, pc=pc, m=m, h=h):
                        j = 128 * (a % 16)
                        for cs_ in range(4):
                            idx = m * 4 + cs_
                            st = 4 * pc + cs_ + 8 * h
                            ctx.wait(sync, "a_res_%d" % idx)
                            ctx.inc(sync.dma_start(
                                out=ctx.xstream[st * 128:(st + 1) * 128, j:j + 128],
                                in_=ob4[:, cs_ * 128:(cs_ + 1) * 128]), "dout%d" % cs_, 16)
                            ctx.mark("a_ost_%d" % idx, "dout%d" % cs_)
                    block.sync(a_out_st)

        ctx.mark("phaseA_done", "dout0")
    ctx.barrier()


TRUNC = 99
B1SUB = 99
DUMP = None


def emit_ffn(ctx, tag):
    if tag == "B":
        _emit_wquant(ctx, tag)
    if TRUNC >= 1:
        _emit_ffn_main(ctx, tag)


def _emit_wquant(ctx, tag):
    nc, block = ctx.nc, ctx.block
    ap = ctx.ap
    P = ctx.P
    ones_t = ctx.ones_t
    pers_sc = ctx.pers_sc
    from contextlib import ExitStack
    with ExitStack() as stack:
        def sb(name, shape, dtype=F32):
            return stack.enter_context(nc.sbuf_tensor("q" + name, list(shape), dtype))
        wb = [sb("_w0", [128, DFF]), sb("_w1", [128, DFF])]
        acc = sb("_acc", [128, 4])
        row = sb("_row", [1, 8])
        qa = sb("_qa", [128, DFF], BF16)
        qb = sb("_qb", [128, DFF], BF16)
        qm = [sb("_qm0", [128, DFF], BF16), sb("_qm1", [128, DFF], BF16)]
        for s_ in ("dq0", "dq1", "dqo0", "dqo1"):
            ctx.sem(s_)

        cfgs = [("w1t", 16, DFF, "m1q"), ("w2t", 64, F, "m2q")]
        flat = [(0, c) for c in range(16)] + [(1, c) for c in range(64)]

        for ci, (wi, c) in enumerate(flat):
            wn, nch, cw, mn = cfgs[wi]
            src = ap[wn].rearrange("(c p) n -> p c n", p=128)
            if True:
                par = ci % 2
                def ld(sync, wi=wi, c=c, ci=ci, par=par, src=src, cw=cw):
                    if ci >= 2:
                        pwi, pc = flat[ci - 2]
                        ctx.wait(sync, "%sqs_%d_%d" % (tag, pwi, pc))
                    ctx.inc(sync.dma_start(out=wb[par][:, 0:cw], in_=src[:, c, :]),
                            "dq%d" % par, 16)
                    ctx.mark("%sqw_%d_%d" % (tag, wi, c), "dq%d" % par)
                block.sync(ld)

                def rd(v, wi=wi, c=c, par=par, cw=cw):
                    ctx.wait(v, "%sqw_%d_%d" % (tag, wi, c))
                    dst = acc[:, 2:3] if c > 0 else acc[:, wi:wi + 1]
                    ctx.inc(v.tensor_reduce(dst, wb[par][:, 0:cw], AX.X, Op.add,
                                            apply_absolute_value=True), "cv")
                    if c > 0:
                        ctx.wait_cnt(v, "cv")
                        ctx.inc(v.tensor_tensor(acc[:, wi:wi + 1], acc[:, wi:wi + 1],
                                                acc[:, 2:3], Op.add), "cv")
                    ctx.mark("%sqs_%d_%d" % (tag, wi, c), "cv")
                ctx.vec(rd)

        def gsum_t(t):
            ctx.wait(t, "%sqs_0_15" % tag)
            ctx.wait(t, "%sqs_1_63" % tag)
            ctx.inc(t.matmul(P[2][0:1, 0:1], acc[:, 0:1], ones_t[:],
                             start=True, stop=True), "ct")
            ctx.inc(t.matmul(P[2][0:1, 1:2], acc[:, 1:2], ones_t[:],
                             start=True, stop=True), "ct")
            ctx.mark("%sqtot" % tag, "ct")
        block.tensor(gsum_t)

        def grow_v(v):
            ctx.wait(v, "%sqtot" % tag)
            ctx.inc(v.tensor_copy(row[0:1, 0:2], P[2][0:1, 0:2]), "cv")
            ctx.wait_cnt(v, "cv")
            for wi in range(2):
                ctx.inc(v.tensor_scalar(row[0:1, 2 + wi:3 + wi], row[0:1, wi:wi + 1],
                                        1.0 / (F * DFF), 1e-5, Op.mult, Op.max), "cv")
            ctx.wait_cnt(v, "cv")
            for wi in range(2):
                ctx.inc(v.reciprocal(row[0:1, 4 + wi:5 + wi], row[0:1, 2 + wi:3 + wi]), "cv")
                ctx.inc(v.tensor_scalar(row[0:1, 6 + wi:7 + wi], row[0:1, 2 + wi:3 + wi],
                                        0.5, None, Op.mult), "cv")
            ctx.mark("%sqrow" % tag, "cv")
        ctx.vec(grow_v)

        def gbc_g(g):
            ctx.wait(g, "%sqrow" % tag)
            ctx.inc(g.partition_broadcast(pers_sc[:, 0:2], row[0:1, 4:6]), "cg")
            ctx.inc(g.partition_broadcast(pers_sc[:, 2:4], row[0:1, 6:8]), "cg")
            ctx.mark("%sqbc" % tag, "cg")
        ctx.gp(gbc_g)

        def gneg_v(v):
            ctx.wait(v, "%sqbc" % tag)
            ctx.inc(v.tensor_scalar(pers_sc[:, 4:6], pers_sc[:, 2:4], -1.0, None,
                                    Op.mult), "cv")
            ctx.mark("%sqneg" % tag, "cv")
        ctx.vec(gneg_v)

        for ci, (wi, c) in enumerate(flat):
            wn, nch, cw, mn = cfgs[wi]
            src = ap[wn].rearrange("(c p) n -> p c n", p=128)
            dst = ctx.scr[mn].rearrange("(c p) n -> p c n", p=128)
            if True:
                par = ci % 2
                def ld2(sync, wi=wi, c=c, ci=ci, par=par, src=src, cw=cw):
                    if ci >= 2:
                        pwi, pc = flat[ci - 2]
                        ctx.wait(sync, "%sq2s_%d_%d" % (tag, pwi, pc))
                    else:
                        pwi, pc = flat[ci - 2 + len(flat)]
                        ctx.wait(sync, "%sqs_%d_%d" % (tag, pwi, pc))
                    ctx.inc(sync.dma_start(out=wb[par][:, 0:cw], in_=src[:, c, :]),
                            "dq%d" % par, 16)
                    ctx.mark("%sq2w_%d_%d" % (tag, wi, c), "dq%d" % par)
                block.sync(ld2)

                def q2(v, wi=wi, c=c, ci=ci, par=par, cw=cw):
                    ctx.wait(v, "%sq2w_%d_%d" % (tag, wi, c))
                    ctx.wait(v, "%sqneg" % tag)
                    if ci >= 2:
                        pwi, pc = flat[ci - 2]
                        ctx.wait(v, "%sq2o_%d_%d" % (tag, pwi, pc))
                    ctx.inc(v.tensor_scalar(qa[:, 0:cw], wb[par][:, 0:cw],
                                            pers_sc[:, 2 + wi:3 + wi], None, Op.is_ge), "cv")
                    ctx.inc(v.tensor_scalar(qb[:, 0:cw], wb[par][:, 0:cw],
                                            pers_sc[:, 4 + wi:5 + wi], None, Op.is_le), "cv")
                    ctx.wait_cnt(v, "cv")
                    ctx.inc(v.tensor_tensor(qm[par][:, 0:cw], qa[:, 0:cw],
                                            qb[:, 0:cw], Op.subtract), "cv")
                    ctx.mark("%sq2s_%d_%d" % (tag, wi, c), "cv")
                ctx.vec(q2)

                def st2(sync, wi=wi, c=c, par=par, dst=dst, cw=cw):
                    ctx.wait(sync, "%sq2s_%d_%d" % (tag, wi, c))
                    ctx.inc(sync.dma_start(out=dst[:, c, :], in_=qm[par][:, 0:cw]),
                            "dqo%d" % par, 16)
                    ctx.mark("%sq2o_%d_%d" % (tag, wi, c), "dqo%d" % par)
                block.sync(st2)
        ctx.mark("%s_wq_done0" % tag, "dqo0")
        ctx.mark("%s_wq_done1" % tag, "dqo1")
    ctx.barrier()


def _emit_ffn_main(ctx, tag):
    nc, block = ctx.nc, ctx.block
    ap = ctx.ap
    P = ctx.P
    PB = ctx.PB
    pers_sc = ctx.pers_sc
    first = (tag == "B")
    from contextlib import ExitStack

    NT = 16

    with ExitStack() as stack:
        def sb(name, shape, dtype=F32):
            return stack.enter_context(nc.sbuf_tensor(tag + name, list(shape), dtype))
        nxh = stack.enter_context(nc.sbuf_tensor(tag + "_nxh", [128, 16 * F], BF16))
        mu_a = sb("_mu", [128, NT])
        fac_a = sb("_fac", [128, NT])
        u1_a = sb("_u1", [128, NT])
        c1sq_a = sb("_c1sq", [128, NT])
        ssq_a = sb("_ssq", [128, NT])
        mth_a = sb("_mth", [128, NT])
        u2_a = sb("_u2", [128, NT])
        d2_a = sb("_d2", [128, NT])
        sc1 = sb("_sc1", [128, 6])

        for s_ in (tag + "ln0", tag + "ln1", tag + "dg1"):
            ctx.sem(s_)

        stB1 = ExitStack()
        def sb1(name, shape, dtype=F32):
            return stB1.enter_context(nc.sbuf_tensor(tag + name, list(shape), dtype))
        bst = sb1("_bst", [128, 24])
        mv = sb1("_mv", [128, 2])
        g1b = sb1("_g1b", [128, F])
        xin = [sb1("_xin0", [128, F]), sb1("_xin1", [128, F])]
        t2 = sb1("_t2", [128, F])
        sgn = sb1("_sgn", [128, F], BF16)
        sgb = sb1("_sgb", [128, F], BF16)
        ni = stB1.enter_context(nc.sbuf_tensor(tag + "_ni", [128, F], I32))
        nbuf = sb1("_nb", [128, F], BF16)

        def f_init(sync):
            ctx.inc(sync.dma_start(out=g1b[0:1, :], in_=ap["g1r"]), tag + "dg1", 16)
            ctx.mark(tag + "_g1row", tag + "dg1")
        block.sync(f_init)

        def f_init_g(g):
            ctx.wait(g, tag + "_g1row")
            ctx.inc(g.partition_broadcast(g1b[:], g1b[0:1, :]), "cg")
            ctx.mark(tag + "_g1b", "cg")
        ctx.gp(f_init_g)

        for t in range(NT):
            par = t % 2

            def l_ld(sync, t=t, par=par):
                if t == 0:
                    ctx.wait_douts(sync)
                lbl = tag + "_xfree_%d" % (t - 2)
                if lbl in ctx.marks:
                    ctx.wait(sync, lbl)
                ctx.inc(sync.dma_start(out=xin[par][:],
                                       in_=ctx.xstream[t * 128:(t + 1) * 128, :]),
                        tag + "ln%d" % par, 16)
                ctx.mark(tag + "_xin_%d" % t, tag + "ln%d" % par)
            block.sync(l_ld)

            def l_stats(v, t=t, par=par):
                ctx.wait(v, tag + "_xin_%d" % t)
                ctx.wait(v, tag + "_g1b")
                for cch in range(4):
                    ctx.inc(v.bn_stats(bst[:, cch * 6:(cch + 1) * 6],
                                       xin[par][:, cch * 512:(cch + 1) * 512]), "cv")
                ctx.wait_cnt(v, "cv")
                ctx.inc(v.bn_aggr(mv[:], bst[:]), "cv")
                ctx.wait_cnt(v, "cv")
                ctx.inc(v.tensor_copy(mu_a[:, t:t + 1], mv[:, 0:1]), "cv")
                ctx.inc(v.tensor_scalar(sc1[:, 0:1], mv[:, 1:2], 1e-5, None, Op.add), "cv")
                ctx.wait_cnt(v, "cv")
                ctx.inc(v.reciprocal(sc1[:, 1:2], sc1[:, 0:1]), "cv")
                ctx.wait_cnt(v, "cv")
                ctx.inc(v.tensor_tensor(sc1[:, 2:3], mv[:, 1:2], sc1[:, 1:2], Op.mult), "cv")
                ctx.wait_cnt(v, "cv")
                ctx.inc(v.tensor_scalar(sc1[:, 2:3], sc1[:, 2:3], 1e-6, None, Op.add), "cv")
                ctx.mark(tag + "_st1_%d" % t, "cv")
            ctx.vec(l_stats)

            def l_sqrt(sc, t=t):
                ctx.wait(sc, tag + "_st1_%d" % t)
                ctx.inc(sc.activation(sc1[:, 3:4], sc1[:, 1:2], AF.Sqrt), "cs")
                ctx.inc(sc.activation(sc1[:, 4:5], sc1[:, 2:3], AF.Sqrt), "cs")
                ctx.mark(tag + "_sq_%d" % t, "cs")
            ctx.scl(l_sqrt)

            def l_fac(v, t=t, par=par):
                ctx.wait(v, tag + "_sq_%d" % t)
                ctx.inc(v.reciprocal(sc1[:, 5:6], sc1[:, 4:5]), "cv")
                ctx.wait_cnt(v, "cv")
                ctx.inc(v.tensor_tensor(fac_a[:, t:t + 1], sc1[:, 3:4], sc1[:, 5:6],
                                        Op.mult), "cv")
                ctx.wait_cnt(v, "cv")
                ctx.inc(v.scalar_tensor_tensor(t2[:], xin[par][:], mu_a[:, t:t + 1],
                                               g1b[:], Op.subtract, Op.mult), "cv")
                ctx.mark(tag + "_xfree_%d" % t, "cv")
                ctx.wait_cnt(v, "cv")
                ctx.inc(v.tensor_reduce(sc1[:, 0:1], t2[:], AX.X, Op.max,
                                        apply_absolute_value=True), "cv")
                ctx.wait_cnt(v, "cv")
                ctx.inc(v.tensor_tensor(sc1[:, 1:2], sc1[:, 0:1], fac_a[:, t:t + 1],
                                        Op.mult), "cv")
                ctx.wait_cnt(v, "cv")
                ctx.inc(v.tensor_scalar(sc1[:, 1:2], sc1[:, 1:2], 1e-5, None, Op.max), "cv")
                ctx.wait_cnt(v, "cv")
                ctx.inc(v.reciprocal(sc1[:, 2:3], sc1[:, 1:2]), "cv")
                ctx.wait_cnt(v, "cv")
                ctx.inc(v.tensor_scalar(sc1[:, 2:3], sc1[:, 2:3], 127.0, None, Op.mult), "cv")
                ctx.wait_cnt(v, "cv")
                ctx.inc(v.tensor_tensor(u1_a[:, t:t + 1], sc1[:, 2:3], fac_a[:, t:t + 1],
                                        Op.mult), "cv")
                ctx.inc(v.tensor_tensor(sc1[:, 3:4], sc1[:, 2:3], pers_sc[:, 0:1],
                                        Op.mult), "cv")
                ctx.wait_cnt(v, "cv")
                ctx.inc(v.reciprocal(sc1[:, 4:5], sc1[:, 3:4]), "cv")
                ctx.wait_cnt(v, "cv")
                ctx.inc(v.tensor_tensor(c1sq_a[:, t:t + 1], sc1[:, 4:5], sc1[:, 4:5],
                                        Op.mult), "cv")
                ctx.mark(tag + "_u1_%d" % t, "cv")
            ctx.vec(l_fac)

            if B1SUB < 1:
                continue

            def l_quant(v, t=t):
                ctx.wait(v, tag + "_u1_%d" % t)
                u1s = u1_a[:, t:t + 1]
                ctx.inc(v.tensor_scalar(ni[:], t2[:], u1s, None, Op.mult), "cv")
                ctx.wait_cnt(v, "cv")
                ctx.inc(v.scalar_tensor_tensor(t2[:], t2[:], u1s, ni[:],
                                               Op.mult, Op.subtract), "cv")
                ctx.wait_cnt(v, "cv")
                ctx.inc(v.tensor_scalar(sgn[:], t2[:], 0.5, None, Op.is_gt), "cv")
                ctx.inc(v.tensor_scalar(sgb[:], t2[:], -0.5, None, Op.is_lt), "cv")
                ctx.wait_cnt(v, "cv")
                ctx.inc(v.scalar_tensor_tensor(t2[:], sgn[:], 1.0, ni[:],
                                               Op.mult, Op.add), "cv")
                ctx.wait_cnt(v, "cv")
                ctx.inc(v.scalar_tensor_tensor(nbuf[:], sgb[:], -1.0, t2[:],
                                               Op.mult, Op.add), "cv")
                ctx.mark(tag + "_nb_%d" % t, "cv")
            ctx.vec(l_quant)

            if B1SUB < 2:
                continue
            for r in range(4):
                def l_tp(te, t=t, r=r):
                    if r == 0:
                        ctx.wait(te, tag + "_nb_%d" % t)
                    if t == 0 and r == 0:
                        ctx.wait(te, "identb_done")
                    half = (t * 4 + r) % 2
                    lbl = tag + "_ev_%d" % (t * 4 + r - 2)
                    if lbl in ctx.marks:
                        ctx.wait(te, lbl)
                    for c4 in range(4):
                        k = r * 4 + c4
                        ctx.inc(te.transpose(
                            PB[half][:, c4 * 128:(c4 + 1) * 128],
                            nbuf[:, k * 128:(k + 1) * 128], ctx.identb[:]), "ct")
                    ctx.mark(tag + "_tp_%d" % (t * 4 + r), "ct")
                block.tensor(l_tp)

                def l_ev(v, t=t, r=r):
                    half = (t * 4 + r) % 2
                    ctx.wait(v, tag + "_tp_%d" % (t * 4 + r))
                    dst = nxh[:].rearrange("p (k n) -> p k n", n=F)
                    ctx.inc(v.tensor_copy(
                        dst[:, r * 4:(r + 1) * 4, t * 128:(t + 1) * 128],
                        PB[half][:, 0:512].rearrange(
                            "p (k n) -> p k n", n=128)), "cv")
                    ctx.mark(tag + "_ev_%d" % (t * 4 + r), "cv")
                ctx.vec(l_ev)
        ctx.mark(tag + "_nxT_done", "cv")
        if DUMP == "stats" and tag == "B":
            def dump_s(sync):
                for e in ("cv", "cs", "ct", "cg"):
                    ctx.wait_cnt(sync, e)
                for col, arr in enumerate([mu_a, fac_a, u1_a, c1sq_a]):
                    ctx.inc(sync.dma_start(out=ctx.out[col * 128:(col + 1) * 128, 0:16],
                                           in_=arr[:]), "dfin", 16)
            self_ = None
            ctx.block.sync(dump_s)
        if DUMP == "nxh" and tag == "B":
            def dump_g(g):
                ctx.wait(g, tag + "_nxT_done")
                ctx.inc(g.dma_start(
                    out=ctx.out.rearrange("(k p) n -> p k n", p=128),
                    in_=nxh[:].rearrange("p (k n) -> p k n", n=F)), "dfin", 16)
            ctx.gp(dump_g)
        ctx.barrier()
        stB1.close()
        if B1SUB < 99:
            return

        if TRUNC < 2:
            return
        with ExitStack() as st2:
            def sb2(name, shape, dtype=F32):
                return st2.enter_context(nc.sbuf_tensor(tag + name, list(shape), dtype))
            m1b = [st2.enter_context(nc.sbuf_tensor(tag + "_m1b0", [128, 16 * 512], BF16)),
                   st2.enter_context(nc.sbuf_tensor(tag + "_m1b1", [128, 16 * 512], BF16))]
            g2b = sb2("_g2b", [128, DFF])
            hb = [sb2("_hb0", [128, 512]), sb2("_hb1", [128, 512])]
            hsq = [sb2("_hq0", [128, 512]), sb2("_hq1", [128, 512])]
            thb = [sb2("_th0", [128, 512]), sb2("_th1", [128, 512])]
            sspart = sb2("_sspart", [128, 2])

            for s_ in (tag + "dm0", tag + "dm1", tag + "dg2", tag + "dth0", tag + "dth1"):
                ctx.sem(s_)

            def h_init(sync):
                ctx.inc(sync.dma_start(out=g2b[0:1, :], in_=ap["g2r"]), tag + "dg2", 16)
                ctx.mark(tag + "_g2row", tag + "dg2")
            block.sync(h_init)

            def h_init_g(g):
                ctx.wait(g, tag + "_g2row")
                ctx.inc(g.partition_broadcast(g2b[:], g2b[0:1, :]), "cg")
                ctx.mark(tag + "_g2b", "cg")
            ctx.gp(h_init_g)

            m1src = ctx.scr["m1q"].rearrange("(k p) n -> p k n", p=128)
            for j in range(16):
                par = j % 2

                def h_ldw(sync, j=j, par=par):
                    lbl = tag + "_hmm_%d" % ((j - 2) * 16 + 15)
                    if lbl in ctx.marks:
                        ctx.wait(sync, lbl)
                    if j < 2 and "B_wq_done0" in ctx.marks:
                        ctx.wait(sync, "B_wq_done0")
                        ctx.wait(sync, "B_wq_done1")
                    ctx.inc(sync.dma_start(
                        out=m1b[par][:].rearrange("p (k n) -> p k n", n=512),
                        in_=m1src[:, :, j * 512:(j + 1) * 512]), tag + "dm%d" % par, 16)
                    ctx.mark(tag + "_m1_%d" % j, tag + "dm%d" % par)
                block.sync(h_ldw)

                for i in range(16):
                    ji = j * 16 + i
                    bank = P[ji % 2]

                    def h_mm(te, j=j, i=i, ji=ji, par=par, bank=bank):
                        ctx.wait(te, tag + "_m1_%d" % j)
                        if j == 0:
                            ctx.wait(te, tag + "_nxT_done")
                        lbl = tag + "_hrelu_%d" % (ji - 2)
                        if lbl in ctx.marks:
                            ctx.wait(te, lbl)
                        nx = nxh[:].rearrange("p (k n) -> p k n", n=F)
                        for k in range(16):
                            ctx.inc(te.matmul(
                                bank[:], nx[:, k, i * 128:(i + 1) * 128],
                                m1b[par][:, k * 512:(k + 1) * 512],
                                start=(k == 0), stop=(k == 15)), "ct")
                        ctx.mark(tag + "_hmm_%d" % ji, "ct")
                    block.tensor(h_mm)

                    def h_relu(v, ji=ji, bank=bank):
                        ctx.wait(v, tag + "_hmm_%d" % ji)
                        hpar = ji % 2
                        lbl = tag + "_hsq_%d" % (ji - 2)
                        if lbl in ctx.marks:
                            ctx.wait(v, lbl)
                        lbl = tag + "_th_%d" % (ji - 2)
                        if lbl in ctx.marks:
                            ctx.wait(v, lbl)
                        ctx.inc(v.tensor_scalar(hb[hpar][:], bank[:], 0.0, None,
                                                Op.max), "cv")
                        ctx.mark(tag + "_hrelu_%d" % ji, "cv")
                    ctx.vec(h_relu)

                    def h_sq(sc, j=j, i=i, ji=ji):
                        ctx.wait(sc, tag + "_hrelu_%d" % ji)
                        hpar = ji % 2
                        lbl = tag + "_hss_%d" % (ji - 1)
                        if lbl in ctx.marks:
                            ctx.wait(sc, lbl)
                        lbl = tag + "_th_%d" % (ji - 2)
                        if lbl in ctx.marks:
                            ctx.wait(sc, lbl)
                        ctx.inc(sc.activation(hsq[hpar][:], hb[hpar][:], AF.Square), "cs")
                        ctx.wait_cnt(sc, "cs")
                        ctx.inc(sc.activation(hb[hpar][:], hsq[hpar][:], AF.Square,
                                              accum_out=sspart[:, 0:1]), "cs")
                        ctx.mark(tag + "_hsq_%d" % ji, "cs")
                    ctx.scl(h_sq)

                    def h_acc(v, j=j, i=i, ji=ji):
                        ctx.wait(v, tag + "_hsq_%d" % ji)
                        hpar = ji % 2
                        if j == 0:
                            ctx.inc(v.tensor_copy(ssq_a[:, i:i + 1], sspart[:, 0:1]), "cv")
                        else:
                            ctx.inc(v.tensor_tensor(ssq_a[:, i:i + 1], ssq_a[:, i:i + 1],
                                                    sspart[:, 0:1], Op.add), "cv")
                        ctx.mark(tag + "_hss_%d" % ji, "cv")
                        ctx.wait(v, tag + "_g2b")
                        lbl = tag + "_tho_%d" % (ji - 2)
                        if lbl in ctx.marks:
                            ctx.wait(v, lbl)
                        ctx.inc(v.tensor_tensor(thb[hpar][:], hsq[hpar][:],
                                                g2b[:, j * 512:(j + 1) * 512], Op.mult), "cv")
                        ctx.mark(tag + "_th_%d" % ji, "cv")
                        ctx.wait_cnt(v, "cv")
                        ctx.inc(v.tensor_reduce(sspart[:, 1:2], thb[hpar][:], AX.X,
                                                Op.max, apply_absolute_value=True), "cv")
                        ctx.wait_cnt(v, "cv")
                        if j == 0:
                            ctx.inc(v.tensor_copy(mth_a[:, i:i + 1], sspart[:, 1:2]), "cv")
                        else:
                            ctx.inc(v.tensor_tensor(mth_a[:, i:i + 1], mth_a[:, i:i + 1],
                                                    sspart[:, 1:2], Op.max), "cv")
                        ctx.mark(tag + "_thm_%d" % ji, "cv")
                    ctx.vec(h_acc)

                    def h_st(sync, j=j, i=i, ji=ji):
                        ctx.wait(sync, tag + "_thm_%d" % ji)
                        hpar = ji % 2
                        ctx.inc(sync.dma_start(
                            out=ctx.scr["th_scr"][i * 128:(i + 1) * 128,
                                                  j * 512:(j + 1) * 512],
                            in_=thb[hpar][:]), tag + "dth%d" % hpar, 16)
                        ctx.mark(tag + "_tho_%d" % ji, tag + "dth%d" % hpar)
                    block.sync(h_st)

            def h_fin_v(v):
                ctx.wait(v, tag + "_thm_255")
                ctx.inc(v.tensor_tensor(u2_a[:], c1sq_a[:], c1sq_a[:], Op.mult), "cv")
                ctx.wait_cnt(v, "cv")
                ctx.inc(v.tensor_tensor(u2_a[:], u2_a[:], ssq_a[:], Op.mult), "cv")
                ctx.wait_cnt(v, "cv")
                ctx.inc(v.tensor_scalar(u2_a[:], u2_a[:], 1.0 / DFF, 1e-6,
                                        Op.mult, Op.add), "cv")
                ctx.mark(tag + "_rms2a_%s" % tag, "cv")
            ctx.vec(h_fin_v)

            def h_fin_s(sc):
                ctx.wait(sc, tag + "_rms2a_%s" % tag)
                ctx.inc(sc.activation(u2_a[:], u2_a[:], AF.Sqrt), "cs")
                ctx.mark(tag + "_rms2_%s" % tag, "cs")
            ctx.scl(h_fin_s)

            def h_fin_v2(v):
                ctx.wait(v, tag + "_rms2_%s" % tag)
                ctx.inc(v.reciprocal(d2_a[:], u2_a[:]), "cv")
                ctx.wait_cnt(v, "cv")
                ctx.inc(v.tensor_tensor(u2_a[:], mth_a[:], c1sq_a[:], Op.mult), "cv")
                ctx.wait_cnt(v, "cv")
                ctx.inc(v.tensor_tensor(u2_a[:], u2_a[:], d2_a[:], Op.mult), "cv")
                ctx.wait_cnt(v, "cv")
                ctx.inc(v.tensor_scalar(u2_a[:], u2_a[:], 1e-5, None, Op.max), "cv")
                ctx.wait_cnt(v, "cv")
                ctx.inc(v.reciprocal(mth_a[:], u2_a[:]), "cv")
                ctx.wait_cnt(v, "cv")
                ctx.inc(v.tensor_scalar(mth_a[:], mth_a[:], 127.0, None, Op.mult), "cv")
                ctx.wait_cnt(v, "cv")
                ctx.inc(v.tensor_tensor(u2_a[:], c1sq_a[:], mth_a[:], Op.mult), "cv")
                ctx.wait_cnt(v, "cv")
                ctx.inc(v.tensor_tensor(u2_a[:], u2_a[:], d2_a[:], Op.mult), "cv")
                ctx.wait_cnt(v, "cv")
                ctx.inc(v.tensor_scalar(d2_a[:], mth_a[:], pers_sc[:, 1:2], None,
                                        Op.mult), "cv")
                ctx.wait_cnt(v, "cv")
                ctx.inc(v.reciprocal(ssq_a[:], d2_a[:]), "cv")
                ctx.wait_cnt(v, "cv")
                ctx.inc(v.tensor_copy(d2_a[:], ssq_a[:]), "cv")
                ctx.mark(tag + "_scal2", "cv")
            ctx.vec(h_fin_v2)

        ctx.barrier()
        if TRUNC < 3:
            return
        with ExitStack() as st3:
            def sb3(name, shape, dtype=F32):
                return st3.enter_context(nc.sbuf_tensor(tag + name, list(shape), dtype))
            thin = sb3("_thin", [128, DFF])
            ab2 = sb3("_ab2", [128, DFF], BF16)
            ab3 = sb3("_ab3", [128, DFF], BF16)
            ni2 = st3.enter_context(nc.sbuf_tensor(tag + "_ni2", [128, DFF], I32))
            nb2 = sb3("_nb2", [128, DFF], BF16)
            m2b = [st3.enter_context(nc.sbuf_tensor(tag + "_m2b0", [128, 8 * 512], BF16)),
                   st3.enter_context(nc.sbuf_tensor(tag + "_m2b1", [128, 8 * 512], BF16))]
            res = [sb3("_res0", [128, 512]), sb3("_res1", [128, 512])]
            ob = [sb3("_ob0", [128, 512]), sb3("_ob1", [128, 512])]

            for s_ in (tag + "dthr", tag + "dm2a", tag + "dm2b", tag + "dres0", tag + "dres1"):
                ctx.sem(s_)

            m2src = ctx.scr["m2q"].rearrange("(k p) n -> p k n", p=128)
            dest = ctx.xstream if first else ctx.out

            for grp in range(4):
                for li in range(4):
                    i = grp * 4 + li

                    def r_ld(sync, grp=grp, li=li, i=i):
                        lbl = tag + "_nb2_%d" % (i - 1)
                        if lbl in ctx.marks:
                            ctx.wait(sync, lbl)
                        if grp == 0 and li == 0:
                            ctx.wait_cnt(sync, tag + "dth0")
                            ctx.wait_cnt(sync, tag + "dth1")
                        ctx.inc(sync.dma_start(
                            out=thin[:], in_=ctx.scr["th_scr"][i * 128:(i + 1) * 128, :]),
                            tag + "dthr", 16)
                        ctx.mark(tag + "_thin_%d" % i, tag + "dthr")
                    block.sync(r_ld)

                    def r_quant(v, i=i):
                        ctx.wait(v, tag + "_thin_%d" % i)
                        ctx.wait(v, tag + "_scal2")
                        lbl = tag + "_nb2_%d" % (i - 1)
                        if lbl in ctx.marks:
                            ctx.wait(v, lbl)
                        u2s = u2_a[:, i:i + 1]
                        ctx.inc(v.tensor_scalar(ni2[:], thin[:], u2s, None, Op.mult), "cv")
                        ctx.wait_cnt(v, "cv")
                        ctx.inc(v.scalar_tensor_tensor(thin[:], thin[:], u2s, ni2[:],
                                                       Op.mult, Op.subtract), "cv")
                        ctx.wait_cnt(v, "cv")
                        ctx.inc(v.tensor_scalar(ab2[:], thin[:], 0.5, None, Op.is_gt), "cv")
                        ctx.inc(v.tensor_scalar(ab3[:], thin[:], -0.5, None, Op.is_lt), "cv")
                        ctx.wait_cnt(v, "cv")
                        ctx.inc(v.scalar_tensor_tensor(thin[:], ab2[:], 1.0, ni2[:],
                                                       Op.mult, Op.add), "cv")
                        ctx.mark(tag + "_ni2_%d" % i, "cv")
                        ctx.wait_cnt(v, "cv")
                        ctx.inc(v.scalar_tensor_tensor(nb2[:], ab3[:], -1.0, thin[:],
                                                       Op.mult, Op.add), "cv")
                        ctx.mark(tag + "_nb2_%d" % i, "cv")
                    ctx.vec(r_quant)

                    for r in range(16):
                        def r_tp(te, grp=grp, li=li, i=i, r=r):
                            if r == 0:
                                ctx.wait(te, tag + "_nb2_%d" % i)
                            if grp == 0 and li == 0 and r == 0:
                                ctx.wait_cnt(te, "cv")
                            rr = i * 16 + r
                            half = rr % 2
                            lbl = tag + "_ev2_%d" % (rr - 2)
                            if lbl in ctx.marks:
                                ctx.wait(te, lbl)
                            for c4 in range(4):
                                k = r * 4 + c4
                                ctx.inc(te.transpose(
                                    PB[half][:, c4 * 128:(c4 + 1) * 128],
                                    nb2[:, k * 128:(k + 1) * 128], ctx.identb[:]), "ct")
                            ctx.mark(tag + "_tp2_%d" % rr, "ct")
                        block.tensor(r_tp)

                        def r_ev(v, grp=grp, li=li, i=i, r=r):
                            nh = nxh[:].rearrange("p (k n) -> p k n", n=512)
                            rr = i * 16 + r
                            half = rr % 2
                            ctx.wait(v, tag + "_tp2_%d" % rr)
                            ctx.inc(v.tensor_copy(
                                nh[:, r * 4:(r + 1) * 4, li * 128:(li + 1) * 128],
                                PB[half][:, 0:512].rearrange(
                                    "p (k n) -> p k n", n=128)), "cv")
                            ctx.mark(tag + "_ev2_%d" % rr, "cv")
                        ctx.vec(r_ev)

                for f in range(4):
                    for kg in range(8):
                        par = kg % 2

                        def m2_ld(sync, grp=grp, f=f, kg=kg, par=par):
                            mseq = grp * 32 + f * 8 + kg
                            lbl = tag + "_m2u_%d" % (mseq - 2)
                            if lbl in ctx.marks:
                                ctx.wait(sync, lbl)
                            ctx.inc(sync.dma_start(
                                out=m2b[par][:].rearrange("p (k n) -> p k n", n=512),
                                in_=m2src[:, kg * 8:(kg + 1) * 8, f * 512:(f + 1) * 512]),
                                tag + "dm2%s" % ("a" if par == 0 else "b"), 16)
                            ctx.mark(tag + "_m2_%d_%d_%d" % (grp, f, kg), tag + "dm2%s" % ("a" if par == 0 else "b"))
                        block.sync(m2_ld)

                        def m2_mm(te, grp=grp, f=f, kg=kg, par=par):
                            ctx.wait(te, tag + "_m2_%d_%d_%d" % (grp, f, kg))
                            if kg == 0:
                                ctx.wait(te, tag + "_ev2_%d" % (grp * 64 + 63))
                            nh = nxh[:].rearrange("p (k n) -> p k n", n=512)
                            for li in range(4):
                                bank = P[2 + li]
                                if kg == 0:
                                    lbl = tag + "_oev_%d_%d" % (grp, (f - 1) * 4 + li)
                                    if lbl in ctx.marks:
                                        ctx.wait(te, lbl)
                                    if grp > 0 and f == 0:
                                        lbl = tag + "_oev_%d_%d" % (grp - 1, 12 + li)
                                        if lbl in ctx.marks:
                                            ctx.wait(te, lbl)
                                for kk in range(8):
                                    k = kg * 8 + kk
                                    ctx.inc(te.matmul(
                                        bank[:], nh[:, k, li * 128:(li + 1) * 128],
                                        m2b[par][:, kk * 512:(kk + 1) * 512],
                                        start=(k == 0), stop=(k == 63)), "ct")
                            mseq = grp * 32 + f * 8 + kg
                            ctx.mark(tag + "_m2u_%d" % mseq, "ct")
                            if kg == 7:
                                ctx.mark(tag + "_mm2_%d_%d" % (grp, f), "ct")
                        block.tensor(m2_mm)

                    for li in range(4):
                        i = grp * 4 + li
                        fi = f * 4 + li
                        osq = grp * 16 + fi
                        par = fi % 2

                        def o_ld(sync, grp=grp, f=f, li=li, i=i, fi=fi, osq=osq, par=par):
                            lbl = tag + "_ost_%d" % (osq - 2)
                            if lbl in ctx.marks:
                                ctx.wait(sync, lbl)
                            ctx.inc(sync.dma_start(
                                out=res[par][:],
                                in_=ctx.xstream[i * 128:(i + 1) * 128,
                                                f * 512:(f + 1) * 512]),
                                tag + "dres%d" % par, 16)
                            ctx.mark(tag + "_res_%d" % osq, tag + "dres%d" % par)
                        block.sync(o_ld)

                        def o_ev(v, grp=grp, f=f, li=li, i=i, fi=fi, osq=osq, par=par):
                            ctx.wait(v, tag + "_mm2_%d_%d" % (grp, f))
                            ctx.wait(v, tag + "_res_%d" % osq)
                            lbl = tag + "_ost_%d" % (osq - 2)
                            if lbl in ctx.marks:
                                ctx.wait(v, lbl)
                            ctx.inc(v.scalar_tensor_tensor(
                                ob[par][:], P[2 + li][:], d2_a[:, i:i + 1], res[par][:],
                                Op.mult, Op.add), "cv")
                            ctx.mark(tag + "_oev_%d_%d" % (grp, fi), "cv")
                        ctx.vec(o_ev)

                        def o_st(sync, grp=grp, f=f, li=li, i=i, fi=fi, osq=osq, par=par):
                            ctx.wait(sync, tag + "_oev_%d_%d" % (grp, fi))
                            ctx.inc(sync.dma_start(
                                out=dest[i * 128:(i + 1) * 128, f * 512:(f + 1) * 512],
                                in_=ob[par][:]), "dout%d" % (fi % 4), 16)
                            ctx.mark(tag + "_ost_%d" % osq, "dout%d" % (fi % 4))
                        block.sync(o_st)

        ctx.mark(tag + "_done", "cv")
    ctx.barrier()


def emit_attn_b(ctx):
    nc, block = ctx.nc, ctx.block
    ap = ctx.ap
    P = ctx.P
    from contextlib import ExitStack
    NT = 16

    with ExitStack() as st1:
        def sb(name, shape, dtype=F32):
            return st1.enter_context(nc.sbuf_tensor("d" + name, list(shape), dtype))
        cosb = sb("_cosb", [128, NT * 64])
        sinb = sb("_sinb", [128, NT * 64])
        prow = sb("_prow", [1, S])
        ifrow = sb("_ifrow", [1, 64])
        asb = sb("_asb", [128, 64])
        ktile = st1.enter_context(nc.sbuf_tensor("d_ki", [128, 64], I32))
        kf = sb("_kf", [128, 64])
        rtile = sb("_rt", [128, 64])
        art = sb("_art", [128, 64])
        x3T = sb("_x3T", [128, 16 * F])
        bst = sb("_bst", [128, 24])
        mv = sb("_mv", [128, 2])
        sc = sb("_sc", [128, 4])
        xin = [sb("_xin0", [128, F])]
        x3 = sb("_x3", [128, F])
        wbuf = [sb("_wb0", [128, 16 * 512])]
        qkv = [sb("_qkv0", [128, 512]), sb("_qkv1", [128, 512])]
        rtm = sb("_rtm", [128, 4 * 256])

        for s_ in ("dpos", "dx0", "dx1", "dw0", "dw1", "dqk0", "dqk1"):
            ctx.sem(s_)

        def d_pos(sync):
            ctx.wait_douts(sync)
            ctx.inc(sync.dma_start(out=prow[0:1, :], in_=ap["posr"]), "dpos", 16)
            ctx.inc(sync.dma_start(out=ifrow[0:1, :], in_=ap["invf"]), "dpos", 16)
            ctx.mark("d_pos", "dpos")
        block.sync(d_pos)

        for t in range(NT):
            def d_ang(te, t=t):
                ctx.wait(te, "d_pos")
                if t > 0:
                    ctx.wait(te, "d_tab_%d" % (t - 1))
                ctx.inc(te.matmul(P[4][:, 0:64], prow[0:1, t * 128:(t + 1) * 128],
                                  ifrow[0:1, :], start=True, stop=True), "ct")
                ctx.mark("d_ang_%d" % t, "ct")
            block.tensor(d_ang)

            def d_cw(v, t=t):
                ctx.wait(v, "d_ang_%d" % t)
                ctx.inc(v.tensor_copy(asb[:], P[4][:, 0:64]), "cv")
                ctx.wait_cnt(v, "cv")
                ctx.inc(v.tensor_scalar(ktile[:], asb[:], INV_2PI, None, Op.mult), "cv")
                ctx.inc(v.tensor_scalar(art[:], asb[:], INV_2PI, None, Op.mult), "cv")
                ctx.wait_cnt(v, "cv")
                ctx.inc(v.tensor_copy(kf[:], ktile[:]), "cv")
                ctx.wait_cnt(v, "cv")
                ctx.inc(v.tensor_tensor(art[:], art[:], kf[:], Op.subtract), "cv")
                ctx.wait_cnt(v, "cv")
                ctx.inc(v.tensor_scalar(art[:], art[:], 0.0, None, Op.is_lt), "cv")
                ctx.wait_cnt(v, "cv")
                ctx.inc(v.tensor_tensor(kf[:], kf[:], art[:], Op.subtract), "cv")
                ctx.wait_cnt(v, "cv")
                ctx.inc(v.cody_waite_cascade(rtile[:], asb[:], kf[:],
                                             CW_C1, CW_C2, CW_C3), "cv")
                ctx.wait_cnt(v, "cv")
                ctx.inc(v.tensor_scalar(art[:], rtile[:], PI_F, None, Op.is_gt), "cv")
                ctx.wait_cnt(v, "cv")
                ctx.inc(v.scalar_tensor_tensor(rtile[:], art[:], -TWO_PI_F, rtile[:],
                                               Op.mult, Op.add), "cv")
                ctx.wait_cnt(v, "cv")
                ctx.inc(v.tensor_scalar(art[:], rtile[:], -1.0, None, Op.mult), "cv")
                ctx.wait_cnt(v, "cv")
                ctx.inc(v.tensor_tensor(art[:], rtile[:], art[:], Op.max), "cv")
                ctx.mark("d_cw_%d" % t, "cv")
            ctx.vec(d_cw)

            def d_sin(s, t=t):
                ctx.wait(s, "d_cw_%d" % t)
                ctx.inc(s.activation(sinb[:, t * 64:(t + 1) * 64], rtile[:], AF.Sin), "cs")
                if t == 0:
                    ctx.wait(s, "hpi_done")
                ctx.inc(s.activation(cosb[:, t * 64:(t + 1) * 64], art[:], AF.Sin,
                                     scale=-1.0, bias=ctx.cst_hpi[:, 0:1]), "cs")
                ctx.mark("d_tab_%d" % t, "cs")
            ctx.scl(d_sin)

        for t in range(NT):
            par = 0

            def d_ld(sync, t=t, par=par):
                lbl = "d_x3_%d" % (t - 1)
                if lbl in ctx.marks:
                    ctx.wait(sync, lbl)
                ctx.inc(sync.dma_start(out=xin[par][:],
                                       in_=ctx.xstream[t * 128:(t + 1) * 128, :]),
                        "dx%d" % par, 16)
                ctx.mark("d_xin_%d" % t, "dx%d" % par)
            block.sync(d_ld)

            def d_ln(v, t=t, par=par):
                ctx.wait(v, "d_xin_%d" % t)
                for cch in range(4):
                    ctx.inc(v.bn_stats(bst[:, cch * 6:(cch + 1) * 6],
                                       xin[par][:, cch * 512:(cch + 1) * 512]), "cv")
                ctx.wait_cnt(v, "cv")
                ctx.inc(v.bn_aggr(mv[:], bst[:]), "cv")
                ctx.wait_cnt(v, "cv")
                ctx.inc(v.tensor_scalar(sc[:, 0:1], mv[:, 1:2], 1e-5, None, Op.add), "cv")
                ctx.wait_cnt(v, "cv")
                ctx.inc(v.reciprocal(sc[:, 1:2], sc[:, 0:1]), "cv")
                ctx.mark("d_r_%d" % t, "cv")
            ctx.vec(d_ln)

            def d_sq(s, t=t):
                ctx.wait(s, "d_r_%d" % t)
                ctx.inc(s.activation(sc[:, 2:3], sc[:, 1:2], AF.Sqrt), "cs")
                ctx.mark("d_iv_%d" % t, "cs")
            ctx.scl(d_sq)

            def d_x3(v, t=t, par=par):
                ctx.wait(v, "d_iv_%d" % t)
                ctx.inc(v.tensor_scalar(x3[:], xin[par][:], mv[:, 0:1], sc[:, 2:3],
                                        Op.subtract, Op.mult), "cv")
                ctx.mark("d_x3_%d" % t, "cv")
            ctx.vec(d_x3)

            for r in range(4):
                def d_tp(te, t=t, r=r):
                    if r == 0:
                        ctx.wait(te, "d_x3_%d" % t)
                    if t == 0 and r == 0:
                        ctx.wait(te, "d_cw_15")
                    half = (t * 4 + r) % 2
                    bank = P[4 + half]
                    lbl = "d_ev_%d" % (t * 4 + r - 2)
                    if lbl in ctx.marks:
                        ctx.wait(te, lbl)
                    for c4 in range(4):
                        k = r * 4 + c4
                        ctx.inc(te.transpose(bank[:, c4 * 128:(c4 + 1) * 128],
                                             x3[:, k * 128:(k + 1) * 128],
                                             ctx.identt[:]), "ct")
                    ctx.mark("d_tp_%d" % (t * 4 + r), "ct")
                block.tensor(d_tp)

                def d_ev(v, t=t, r=r):
                    half = (t * 4 + r) % 2
                    bank = P[4 + half]
                    ctx.wait(v, "d_tp_%d" % (t * 4 + r))
                    dst = x3T[:].rearrange("p (k n) -> p k n", n=F)
                    ctx.inc(v.tensor_copy(
                        dst[:, r * 4:(r + 1) * 4, t * 128:(t + 1) * 128],
                        bank[:].rearrange("p (k n) -> p k n", n=128)), "cv")
                    ctx.mark("d_ev_%d" % (t * 4 + r), "cv")
                ctx.vec(d_ev)
        ctx.mark("d_x3T_done", "cv")

        wnames = ["wqt", "wkt", "wvt"]
        snames = ["q_scr", "k_scr", "v_scr"]
        for wi in range(3):
            wsrc = ap[wnames[wi]].rearrange("(k p) n -> p k n", p=128)
            for o in range(4):
                wo_ = wi * 4 + o
                par = 0

                def p_ldw(sync, wi=wi, o=o, wo_=wo_, par=par, wsrc=wsrc):
                    lbl = "d_mm_%d_%d" % (wo_ - 1, NT - 1)
                    if lbl in ctx.marks:
                        ctx.wait(sync, lbl)
                    ctx.inc(sync.dma_start(
                        out=wbuf[par][:].rearrange("p (k n) -> p k n", n=512),
                        in_=wsrc[:, :, o * 512:(o + 1) * 512]), "dw%d" % par, 16)
                    ctx.mark("d_w_%d" % wo_, "dw%d" % par)
                block.sync(p_ldw)

                for i in range(NT):
                    bank = P[(wo_ * NT + i) % 2]

                    def p_mm(te, wi=wi, o=o, wo_=wo_, i=i, par=par, bank=bank):
                        ctx.wait(te, "d_w_%d" % wo_)
                        if wo_ == 0 and i == 0:
                            ctx.wait(te, "d_x3T_done")
                        lbl = "d_pje_%d" % (wo_ * NT + i - 2)
                        if lbl in ctx.marks:
                            ctx.wait(te, lbl)
                        xv = x3T[:].rearrange("p (k n) -> p k n", n=F)
                        for k in range(16):
                            ctx.inc(te.matmul(bank[:], xv[:, k, i * 128:(i + 1) * 128],
                                              wbuf[par][:, k * 512:(k + 1) * 512],
                                              start=(k == 0), stop=(k == 15)), "ct")
                        ctx.mark("d_mm_%d_%d" % (wo_, i), "ct")
                    block.tensor(p_mm)

                    def p_ev(v, wi=wi, o=o, wo_=wo_, i=i, bank=bank):
                        seq = wo_ * NT + i
                        ctx.wait(v, "d_mm_%d_%d" % (wo_, i))
                        qpar = seq % 2
                        lbl = "d_pjo_%d" % (seq - 2)
                        if lbl in ctx.marks:
                            ctx.wait(v, lbl)
                        if wi == 2:
                            ctx.inc(v.tensor_copy(qkv[qpar][:], bank[:]), "cv")
                        else:
                            pp = bank[:].rearrange("p (h d) -> p h d", d=128)
                            x1 = pp[:, :, 0:128:2]
                            x2 = pp[:, :, 1:128:2]
                            cb = cosb[:, i * 64:(i + 1) * 64].unsqueeze(1).broadcast_to([128, 4, 64])
                            sb_ = sinb[:, i * 64:(i + 1) * 64].unsqueeze(1).broadcast_to([128, 4, 64])
                            rv = rtm[:].rearrange("p (j n) -> p j n", n=256)
                            t1 = rv[:, 0, :].rearrange("p (h d) -> p h d", d=64)
                            t2 = rv[:, 1, :].rearrange("p (h d) -> p h d", d=64)
                            t3 = rv[:, 2, :].rearrange("p (h d) -> p h d", d=64)
                            t4 = rv[:, 3, :].rearrange("p (h d) -> p h d", d=64)
                            ctx.inc(v.tensor_tensor(t1, x1, cb, Op.mult), "cv")
                            ctx.inc(v.tensor_tensor(t2, x2, sb_, Op.mult), "cv")
                            ctx.inc(v.tensor_tensor(t3, x1, sb_, Op.mult), "cv")
                            ctx.inc(v.tensor_tensor(t4, x2, cb, Op.mult), "cv")
                            ctx.wait_cnt(v, "cv")
                            ov = qkv[qpar][:].rearrange("p (h d) -> p h d", d=128)
                            ctx.inc(v.tensor_tensor(ov[:, :, 0:64], t1, t2, Op.subtract), "cv")
                            ctx.inc(v.tensor_tensor(ov[:, :, 64:128], t3, t4, Op.add), "cv")
                        ctx.mark("d_pje_%d" % seq, "cv")
                    ctx.vec(p_ev)

                    def p_st(sync, wi=wi, o=o, wo_=wo_, i=i):
                        seq = wo_ * NT + i
                        qpar = seq % 2
                        ctx.wait(sync, "d_pje_%d" % seq)
                        ctx.inc(sync.dma_start(
                            out=ctx.scr[snames[wi]][i * 128:(i + 1) * 128,
                                                    o * 512:(o + 1) * 512],
                            in_=qkv[qpar][:]), "dqk%d" % qpar, 16)
                        ctx.mark("d_pjo_%d" % seq, "dqk%d" % qpar)
                    block.sync(p_st)
    ctx.barrier()

    with ExitStack() as st2:
        def sb2(name, shape, dtype=F32):
            return st2.enter_context(nc.sbuf_tensor("e" + name, list(shape), dtype))
        qt = sb2("_qt", [128, F])
        kt = sb2("_kt", [128, F])
        vt = sb2("_vt", [128, F])
        prod = sb2("_prod", [128, F])
        scb = sb2("_scb", [128, 256])
        negm = sb2("_negm", [128, 16])
        zz = sb2("_zz", [128, 16])
        attn = sb2("_attn", [128, F])
        attnT = sb2("_attnT", [128, F])

        for s_ in ("dq", "dk", "dv", "dat"):
            ctx.sem(s_)

        for i in range(NT):
            def a_ld(sync, i=i):
                lbl = "e_done_%d" % (i - 1)
                if lbl in ctx.marks:
                    ctx.wait(sync, lbl)
                ctx.inc(sync.dma_start(out=qt[:], in_=ctx.scr["q_scr"][i * 128:(i + 1) * 128, :]), "dq", 16)
                ctx.inc(sync.dma_start(out=kt[:], in_=ctx.scr["k_scr"][i * 128:(i + 1) * 128, :]), "dk", 16)
                ctx.inc(sync.dma_start(out=vt[:], in_=ctx.scr["v_scr"][i * 128:(i + 1) * 128, :]), "dv", 16)
                ctx.mark("e_q_%d" % i, "dq")
                ctx.mark("e_k_%d" % i, "dk")
                ctx.mark("e_v_%d" % i, "dv")
            block.sync(a_ld)

            def a_sc(v, i=i):
                ctx.wait(v, "e_q_%d" % i)
                ctx.wait(v, "e_k_%d" % i)
                qv = qt[:].rearrange("p (h d) -> p h d", d=128)
                sv = scb[:].rearrange("p (h t) -> p h t", t=16)
                for t16 in range(16):
                    kb = kt[:, t16 * 128:(t16 + 1) * 128].unsqueeze(1).broadcast_to([128, 16, 128])
                    ctx.wait_cnt(v, "cv")
                    ctx.inc(v.tensor_tensor(prod[:].rearrange("p (h d) -> p h d", d=128),
                                            qv, kb, Op.mult), "cv")
                    ctx.wait_cnt(v, "cv")
                    ctx.inc(v.tensor_reduce(sv[:, :, t16:t16 + 1],
                                            prod[:].rearrange("p (h d) -> p h d", d=128),
                                            AX.X, Op.add), "cv")
                ctx.wait_cnt(v, "cv")
                ctx.inc(v.tensor_reduce(negm[:], sv, AX.X, Op.max, negate=True), "cv")
                ctx.wait_cnt(v, "cv")
                for h in range(16):
                    ctx.inc(v.tensor_scalar(scb[:, h * 16:(h + 1) * 16],
                                            scb[:, h * 16:(h + 1) * 16],
                                            negm[:, h:h + 1], None, Op.add), "cv")
                ctx.mark("e_sc_%d" % i, "cv")
            ctx.vec(a_sc)

            def a_exp(s, i=i):
                ctx.wait(s, "e_sc_%d" % i)
                ctx.inc(s.activation(scb[:], scb[:], AF.Exp, scale=INV_SQRT_HD), "cs")
                ctx.mark("e_exp_%d" % i, "cs")
            ctx.scl(a_exp)

            def a_pv(v, i=i):
                ctx.wait(v, "e_exp_%d" % i)
                ctx.wait(v, "e_v_%d" % i)
                sv = scb[:].rearrange("p (h t) -> p h t", t=16)
                ctx.inc(v.tensor_reduce(zz[:], sv, AX.X, Op.add), "cv")
                ctx.wait_cnt(v, "cv")
                ctx.inc(v.reciprocal(zz[:], zz[:]), "cv")
                ctx.wait_cnt(v, "cv")
                for h in range(16):
                    ctx.inc(v.tensor_scalar(scb[:, h * 16:(h + 1) * 16],
                                            scb[:, h * 16:(h + 1) * 16],
                                            zz[:, h:h + 1], None, Op.mult), "cv")
                for t16 in range(16):
                    for h in range(16):
                        ctx.wait_cnt(v, "cv")
                        if t16 == 0:
                            ctx.inc(v.tensor_scalar(
                                attn[:, h * 128:(h + 1) * 128],
                                vt[:, 0:128], scb[:, h * 16:h * 16 + 1],
                                None, Op.mult), "cv")
                        else:
                            ctx.inc(v.scalar_tensor_tensor(
                                attn[:, h * 128:(h + 1) * 128],
                                vt[:, t16 * 128:(t16 + 1) * 128],
                                scb[:, h * 16 + t16:h * 16 + t16 + 1],
                                attn[:, h * 128:(h + 1) * 128],
                                Op.mult, Op.add), "cv")
                ctx.mark("e_pv_%d" % i, "cv")
            ctx.vec(a_pv)

            for r in range(4):
                def a_tp(te, i=i, r=r):
                    if r == 0:
                        ctx.wait(te, "e_pv_%d" % i)
                    half = (i * 4 + r) % 2
                    bank = P[4 + half]
                    lbl = "e_ev_%d" % (i * 4 + r - 2)
                    if lbl in ctx.marks:
                        ctx.wait(te, lbl)
                    for c4 in range(4):
                        k = r * 4 + c4
                        ctx.inc(te.transpose(bank[:, c4 * 128:(c4 + 1) * 128],
                                             attn[:, k * 128:(k + 1) * 128],
                                             ctx.identt[:]), "ct")
                    ctx.mark("e_tp_%d" % (i * 4 + r), "ct")
                block.tensor(a_tp)

                def a_ev(v, i=i, r=r):
                    half = (i * 4 + r) % 2
                    bank = P[4 + half]
                    ctx.wait(v, "e_tp_%d" % (i * 4 + r))
                    dst = attnT[:].rearrange("p (k n) -> p k n", n=128)
                    ctx.inc(v.tensor_copy(dst[:, r * 4:(r + 1) * 4, :],
                                          bank[:].rearrange("p (k n) -> p k n", n=128)), "cv")
                    ctx.mark("e_ev_%d" % (i * 4 + r), "cv")
                ctx.vec(a_ev)

            def a_st(sync, i=i):
                ctx.wait(sync, "e_ev_%d" % (i * 4 + 3))
                dst = ctx.scr["at_scr"].rearrange("(k p) n -> p k n", p=128)
                ctx.inc(sync.dma_start(
                    out=dst[:, :, i * 128:(i + 1) * 128],
                    in_=attnT[:].rearrange("p (k n) -> p k n", n=128)), "dat", 16)
                ctx.mark("e_done_%d" % i, "dat")
            block.sync(a_st)
    ctx.barrier()

    with ExitStack() as st3:
        def sb3(name, shape, dtype=F32):
            return st3.enter_context(nc.sbuf_tensor("f" + name, list(shape), dtype))
        wob = [sb3("_wo0", [128, 16 * 512]), sb3("_wo1", [128, 16 * 512])]
        atb = [sb3("_at0", [128, 16 * 128]), sb3("_at1", [128, 16 * 128])]
        res = [sb3("_res0", [128, 512]), sb3("_res1", [128, 512])]
        ob = [sb3("_ob0", [128, 512]), sb3("_ob1", [128, 512])]
        for s_ in ("dwo0", "dwo1", "dab0", "dab1", "drs0", "drs1"):
            ctx.sem(s_)

        wsrc = ap["wot"].rearrange("(k p) n -> p k n", p=128)
        asrc = ctx.scr["at_scr"].rearrange("(k p) n -> p k n", p=128)
        for f in range(4):
            fpar = f % 2

            def w_ld(sync, f=f, fpar=fpar):
                lbl = "f_mm_%d_%d" % (f - 2, NT - 1)
                if lbl in ctx.marks:
                    ctx.wait(sync, lbl)
                ctx.inc(sync.dma_start(
                    out=wob[fpar][:].rearrange("p (k n) -> p k n", n=512),
                    in_=wsrc[:, :, f * 512:(f + 1) * 512]), "dwo%d" % fpar, 16)
                ctx.mark("f_w_%d" % f, "dwo%d" % fpar)
            block.sync(w_ld)

            for i in range(NT):
                seq = f * NT + i
                bank = P[seq % 2]
                apar = seq % 2

                def w_lda(sync, f=f, i=i, seq=seq, apar=apar):
                    lbl = "f_mm_s%d" % (seq - 2)
                    if lbl in ctx.marks:
                        ctx.wait(sync, lbl)
                    ctx.inc(sync.dma_start(
                        out=atb[apar][:].rearrange("p (k n) -> p k n", n=128),
                        in_=asrc[:, :, i * 128:(i + 1) * 128]), "dab%d" % apar, 16)
                    ctx.mark("f_a_%d" % seq, "dab%d" % apar)
                block.sync(w_lda)

                def w_mm(te, f=f, i=i, seq=seq, apar=apar, fpar=fpar, bank=bank):
                    ctx.wait(te, "f_w_%d" % f)
                    ctx.wait(te, "f_a_%d" % seq)
                    lbl = "f_ev_%d" % (seq - 2)
                    if lbl in ctx.marks:
                        ctx.wait(te, lbl)
                    for k in range(16):
                        ctx.inc(te.matmul(bank[:],
                                          atb[apar][:, k * 128:(k + 1) * 128],
                                          wob[fpar][:, k * 512:(k + 1) * 512],
                                          start=(k == 0), stop=(k == 15)), "ct")
                    ctx.mark("f_mm_s%d" % seq, "ct")
                    ctx.mark("f_mm_%d_%d" % (f, i), "ct")
                block.tensor(w_mm)

                def w_res(sync, f=f, i=i, seq=seq):
                    rpar = seq % 2
                    lbl = "f_ev_%d" % (seq - 2)
                    if lbl in ctx.marks:
                        ctx.wait(sync, lbl)
                    ctx.inc(sync.dma_start(
                        out=res[rpar][:],
                        in_=ctx.xstream[i * 128:(i + 1) * 128, f * 512:(f + 1) * 512]),
                        "drs%d" % rpar, 16)
                    ctx.mark("f_r_%d" % seq, "drs%d" % rpar)
                block.sync(w_res)

                def w_ev(v, f=f, i=i, seq=seq, bank=bank):
                    rpar = seq % 2
                    ctx.wait(v, "f_mm_s%d" % seq)
                    ctx.wait(v, "f_r_%d" % seq)
                    lbl = "f_st_%d" % (seq - 2)
                    if lbl in ctx.marks:
                        ctx.wait(v, lbl)
                    ctx.inc(v.scalar_tensor_tensor(ob[rpar][:], bank[:], 1.0,
                                                   res[rpar][:], Op.mult, Op.add), "cv")
                    ctx.mark("f_ev_%d" % seq, "cv")
                ctx.vec(w_ev)

                def w_st(sync, f=f, i=i, seq=seq):
                    rpar = seq % 2
                    ctx.wait(sync, "f_ev_%d" % seq)
                    ctx.inc(sync.dma_start(
                        out=ctx.xstream[i * 128:(i + 1) * 128, f * 512:(f + 1) * 512],
                        in_=ob[rpar][:]), "dout%d" % (seq % 4), 16)
                    ctx.mark("f_st_%d" % seq, "dout%d" % (seq % 4))
                block.sync(w_st)
    ctx.barrier()





def make_inmaps(inputs):
    X = np.ascontiguousarray(np.asarray(inputs["L0_attn_a_X"], np.float32))
    WQ = np.asarray(inputs["L0_attn_a_W_Q"], np.float32)
    hhb = np.asarray(inputs["L0_attn_a_hh_b"], np.float32)
    gnw = np.asarray(inputs["L0_attn_a_gn_weight"], np.float32)
    gnb = np.asarray(inputs["L0_attn_a_gn_bias"], np.float32)
    w1 = np.asarray(inputs["L0_ffn_w1"], np.float32)
    w2 = np.asarray(inputs["L0_ffn_w2"], np.float32)
    g1 = np.asarray(inputs["L0_ffn_g1"], np.float32)
    g2 = np.asarray(inputs["L0_ffn_g2"], np.float32)
    wq = np.asarray(inputs["L0_attn_b_wq"], np.float32)
    wk = np.asarray(inputs["L0_attn_b_wk"], np.float32)
    wv = np.asarray(inputs["L0_attn_b_wv"], np.float32)
    wo = np.asarray(inputs["L0_attn_b_wo"], np.float32)
    invf = np.asarray(inputs["L0_attn_b_inv_freq"], np.float32)

    inv_freq_a = (1.0 / 10000.0 ** (np.arange(0, 64, 2, dtype=np.float32) / 64)).astype(np.float32)
    freqs = np.arange(T, dtype=np.float32)[:, None] * inv_freq_a[None, :]
    ropec0 = np.cos(freqs).astype(np.float32)
    ropes0 = np.sin(freqs).astype(np.float32)
    ropec1 = np.ones((T, 32), np.float32)
    ropes1 = np.zeros((T, 32), np.float32)
    ident = np.eye(128, dtype=np.float32)
    ones128 = np.ones((128, 1), np.float32)

    shared = dict(
        w1t=np.ascontiguousarray(w1.T), w2t=np.ascontiguousarray(w2.T),
        g1r=np.ascontiguousarray(g1.reshape(1, F)),
        g2r=np.ascontiguousarray(g2.reshape(1, DFF)),
        wqt=np.ascontiguousarray(wq.T), wkt=np.ascontiguousarray(wk.T),
        wvt=np.ascontiguousarray(wv.T), wot=np.ascontiguousarray(wo.T),
        invf=np.ascontiguousarray(invf.reshape(1, 64)),
        ident=ident, ones128=ones128,
    )
    in_maps = []
    for cid in range(NCORES):
        b, g = cid // 2, cid % 2
        perm = tau_perm(g)
        m = dict(shared)
        m["xt"] = np.ascontiguousarray(X[b].T)
        m["xres"] = np.ascontiguousarray(X[b][perm])
        m["wqcol"] = np.ascontiguousarray(WQ[:, 1024 * g:1024 * (g + 1)])
        m["hhb"] = np.ascontiguousarray(hhb[1024 * g:1024 * (g + 1)].reshape(1, 1024))
        m["gnw"] = np.ascontiguousarray(gnw[1024 * g:1024 * (g + 1)].reshape(1, 1024))
        m["gnb"] = np.ascontiguousarray(gnb[1024 * g:1024 * (g + 1)].reshape(1, 1024))
        m["ropec"] = ropec0 if g == 0 else ropec1
        m["ropes"] = ropes0 if g == 0 else ropes1
        m["posr"] = np.ascontiguousarray(perm.astype(np.float32).reshape(1, S))
        in_maps.append(m)
    return in_maps


def kernel(**inputs):
    in_maps = make_inmaps(inputs)
    nc = build_nc()
    res = run_bass_kernel_spmd(nc, in_maps, core_ids=list(range(NCORES)))
    full = np.empty((4, T, F), np.float32)
    for cid in range(NCORES):
        b, g = cid // 2, cid % 2
        full[b, tau_perm(g), :] = res.results[cid]["out"]
    return full
